# version 02 (final)
# baseline (speedup 1.0000x reference)
import os
os.environ.setdefault("NEURON_RT_EXEC_TIMEOUT", "900")

import contextlib
import numpy as np
import ml_dtypes
import concourse.bass as bass
import concourse.mybir as mybir
from concourse.bass import IndirectOffsetOnAxis
from concourse.bass_utils import run_bass_kernel_spmd

F32 = mybir.dt.float32
BF16 = mybir.dt.bfloat16
I32 = mybir.dt.int32
AF = mybir.ActivationFunctionType
ALU = mybir.AluOpType
AX = mybir.AxisListType
BF16NP = ml_dtypes.bfloat16

B, S, D = 4, 4096, 2048
HA, DA = 16, 128
HB, KVH, DB = 32, 8, 64
MD, NE, TOPK = 1408, 16, 6
NMC = MD // 128
EPS = 1e-5

N_PHASE = 4
MOE_NE = NE
MOE_NGRP = None
ATTNB_NHC = 32
DBG_IN = ()
DBG = ()

ENGS = ("sync", "gpsimd", "tensor", "vector", "scalar")


class Sem:
    __slots__ = ("h", "n")

    def __init__(self, nc, name):
        self.h = nc.alloc_semaphore(name)
        self.n = 0


class Bld:
    def __init__(self, nc):
        self.nc = nc
        self.steps = {e: [] for e in ENGS}
        self.TT = Sem(nc, "tt")
        self.VV = Sem(nc, "vv")
        self.SS = Sem(nc, "ss")
        self.dsems = {}
        self.dq = {"sync": {}, "gpsimd": {}}
        self.io = {}

    def _add(self, eng, fn):
        self.steps[eng].append(fn)

    def sbuf(self, name, shape, dt):
        self._uid = getattr(self, "_uid", 0) + 1
        return self.nc.sbuf_tensor(f"{name}_{self._uid}", shape, dt)

    def psum(self, name, shape, dt):
        self._uid = getattr(self, "_uid", 0) + 1
        return self.nc.psum_tensor(f"{name}_{self._uid}", shape, dt)

    def wait(self, eng, tok):
        if tok is None:
            return
        sem, v = tok
        if v <= 0:
            return
        self._add(eng, lambda h, s=sem.h, v=v: h.wait_ge(s, v))

    def waits(self, eng, toks):
        best = {}
        for t in toks:
            if t is not None:
                sem, v = t
                if v > 0:
                    best[sem] = max(best.get(sem, 0), v)
        for sem, v in best.items():
            self.wait(eng, (sem, v))

    def _dsem(self, eng, key):
        kk = (eng, key)
        if kk not in self.dsems:
            self.dsems[kk] = Sem(self.nc, f"d{len(self.dsems)}")
        sem = self.dsems[kk]
        self.dq[eng][kk] = sem
        return sem

    def dma(self, eng, out_ap, in_ap, key=None):
        sem = self._dsem(eng, key or ("q_" + eng))
        sem.n += 16
        self._add(eng, lambda h, o=out_ap, i=in_ap, s=sem.h:
                  h.dma_start(out=o, in_=i).then_inc(s, 16))
        return (sem, sem.n)

    def idma(self, eng, out_ap, in_ap, idx_col, key, eoff=0):
        eng = "gpsimd"
        sem = self._dsem(eng, key)
        sem.n += 16
        self._add(eng, lambda h, o=out_ap, i=in_ap, f=idx_col, s=sem.h,
                  eo=eoff:
                  h.indirect_dma_start(
                      out=o, out_offset=None, in_=i,
                      in_offset=IndirectOffsetOnAxis(f, 0),
                      element_offset=eo).then_inc(s, 16))
        return (sem, sem.n)

    def op(self, eng, meth, *args, sem=None, **kw):
        tok = None
        if sem is not None:
            sem.n += 1
            tok = (sem, sem.n)

        def fn(h, meth=meth, args=args, kw=kw, sem=sem):
            ins = getattr(h, meth)(*args, **kw)
            if sem is not None:
                ins.then_inc(sem.h, 1)
        self._add(eng, fn)
        return tok

    def mm(self, psum_ap, lhsT, rhs, start, stop, sem=None):
        tok = None
        if sem is not None:
            sem.n += 1
            tok = (sem, sem.n)

        def fn(h, o=psum_ap, l=lhsT, r=rhs, st=start, sp=stop, sem=sem):
            ins = h.matmul(o, l, r, start=st, stop=sp, skip_group_check=True)
            if sem is not None:
                ins.then_inc(sem.h, 1)
        self._add("tensor", fn)
        return tok

    def tr(self, out_ap, in_ap, sem=None):
        tok = None
        if sem is not None:
            sem.n += 1
            tok = (sem, sem.n)

        def fn(h, o=out_ap, i=in_ap, sem=sem):
            ins = h.transpose(o, i, self.io["ident_sb"][:, :])
            if sem is not None:
                ins.then_inc(sem.h, 1)
        self._add("tensor", fn)
        return tok

    def gfence(self):
        for sem in (self.TT, self.VV, self.SS):
            self.wait("gpsimd", (sem, sem.n))
        for sem in self.dsems.values():
            self.wait("gpsimd", (sem, sem.n))

    def fence(self, engs=ENGS):
        for qeng in ("sync", "gpsimd"):
            if qeng in engs:
                for sem in self.dq[qeng].values():
                    self.wait(qeng, (sem, sem.n))
        for e in engs:
            for sem in (self.TT, self.VV, self.SS):
                self.wait(e, (sem, sem.n))
            for sem in self.dsems.values():
                self.wait(e, (sem, sem.n))

    def finalize(self):
        with self.nc.Block() as block:
            for e in ENGS:
                fns = self.steps[e]

                def runner(h, fns=fns):
                    for fn in fns:
                        fn(h)
                getattr(block, e)(runner)


def ln_transpose(b, ntiles, src_rows, xnT, gating=False, cw=None):
    nc, io = b.nc, b.io
    TT, VV, SS = b.TT, b.VV, b.SS
    with contextlib.ExitStack() as ex:
        xr = ex.enter_context(b.sbuf("lnx", [128, 1, D], F32))
        jnk = ex.enter_context(b.sbuf("lnj", [128, D], F32))
        st = ex.enter_context(b.sbuf("lnst", [128, 8], F32))
        xnb = ex.enter_context(b.sbuf("lnxb", [128, 1, D], BF16))
        pt0 = ex.enter_context(b.psum("lnp0", [128, 512], BF16))
        pt1 = ex.enter_context(b.psum("lnp1", [128, 512], BF16))
        ptr = (pt0, pt1)
        if gating:
            sm = ex.enter_context(b.sbuf("lnsm", [128, 6, 16], F32))
            jnkr = ex.enter_context(b.sbuf("lnjr", [128, NE, 16], F32))
            pl = ex.enter_context(b.psum("lnpl", [128, 16], F32))
        s1, s2, mean, rstd = (st[:, i:i + 1] for i in range(4))
        x_rel = [None, None]
        xnb_rel = [None, None]
        ev_tok = [None, None]
        glog_free = None
        for t in range(ntiles):
            sl = 0
            srcspec = src_rows(t)
            if isinstance(srcspec, tuple) and srcspec[0] == "gather":
                _, hbm, idxcol = srcspec
                b.wait("gpsimd", x_rel[sl])
                ld = b.idma("gpsimd", xr[:, sl, :], hbm, idxcol,
                            key=("lnxg", sl))
            else:
                b.wait("sync", x_rel[sl])
                ld = b.dma("sync", xr[:, sl, :], srcspec, key=("lnx", sl))
            b.wait("vector", ld)
            t_s1 = b.op("vector", "tensor_reduce", s1, xr[:, sl, :], AX.X,
                        ALU.add, sem=VV)
            b.wait("scalar", ld)
            t_s2 = b.op("scalar", "activation", jnk[:, :], xr[:, sl, :],
                        AF.Square, accum_out=s2, sem=SS)
            b.wait("vector", t_s1)
            b.wait("vector", t_s2)
            b.op("vector", "tensor_scalar", mean, s1, 1.0 / D, None, ALU.mult,
                 sem=VV)
            b.wait("vector", (VV, VV.n))
            b.op("vector", "tensor_tensor", st[:, 4:5], mean, mean, ALU.mult,
                 sem=VV)
            b.wait("vector", (VV, VV.n))
            vtok = b.op("vector", "scalar_tensor_tensor", st[:, 5:6], s2,
                        1.0 / D, st[:, 4:5], ALU.mult, ALU.subtract, sem=VV)
            b.wait("scalar", vtok)
            ltok = b.op("scalar", "activation", st[:, 6:7], st[:, 5:6], AF.Ln,
                        bias=io["c_eps"], sem=SS)
            b.wait("scalar", ltok)
            rtok = b.op("scalar", "activation", rstd, st[:, 6:7], AF.Exp,
                        scale=-0.5, sem=SS)
            b.wait("vector", rtok)
            b.waits("vector", [xnb_rel[sl]])
            ntok = b.op("vector", "tensor_scalar", xnb[:, sl, :], xr[:, sl, :],
                        mean, rstd, ALU.subtract, ALU.mult, sem=VV)
            x_rel[sl] = ntok
            b.wait("tensor", ntok)
            for grp in range(4):
                bank = ptr[grp % 2]
                b.wait("tensor", ev_tok[grp % 2])
                tk = None
                for j in range(4):
                    kd = grp * 4 + j
                    tk = b.tr(bank[:, j * 128:(j + 1) * 128],
                              xnb[:, sl, kd * 128:(kd + 1) * 128],
                              sem=(TT if j == 3 else None))
                b.wait("scalar", tk)
                ev_tok[grp % 2] = b.op(
                    "scalar", "copy",
                    xnT[:, grp * 4:grp * 4 + 4, t * 128:(t + 1) * 128],
                    bank[:, :].rearrange("p (kd t) -> p kd t", t=128), sem=SS)
            xnb_rel[sl] = ev_tok[1]
            if gating:
                b.waits("tensor", [ev_tok[0], ev_tok[1], glog_free])
                lt = None
                for kd in range(16):
                    lt = b.mm(pl[:, :], xnT[:, kd, t * 128:(t + 1) * 128],
                              io["gw_sb"][:, kd, :], start=(kd == 0),
                              stop=(kd == 15), sem=(TT if kd == 15 else None))
                b.wait("vector", lt)
                b.op("vector", "tensor_reduce", sm[:, 4, 0:1], pl[:, :], AX.X,
                     ALU.max, sem=VV)
                b.wait("vector", (VV, VV.n))
                nmx = b.op("vector", "tensor_scalar", sm[:, 4, 1:2],
                           sm[:, 4, 0:1], -1.0, None, ALU.mult, sem=VV)
                b.wait("scalar", nmx)
                etok = b.op("scalar", "activation", sm[:, 0, :], pl[:, :],
                            AF.Exp, bias=sm[:, 4, 1:2], sem=SS)
                glog_free = etok
                b.wait("vector", etok)
                b.op("vector", "tensor_reduce", sm[:, 4, 2:3], sm[:, 0, :],
                     AX.X, ALU.add, sem=VV)
                b.wait("vector", (VV, VV.n))
                b.op("vector", "reciprocal", sm[:, 4, 3:4], sm[:, 4, 2:3],
                     sem=VV)
                b.wait("vector", (VV, VV.n))
                b.op("vector", "tensor_scalar", sm[:, 1, :], sm[:, 0, :],
                     sm[:, 4, 3:4], None, ALU.mult, sem=VV)
                b.wait("vector", (VV, VV.n))
                for e in range(NE):
                    b.op("vector", "tensor_scalar", jnkr[:, e, :],
                         sm[:, 1, :], sm[:, 1, e:e + 1], 0.0, ALU.is_gt,
                         ALU.add, accum_out=sm[:, 3, e:e + 1],
                         sem=(VV if e == NE - 1 else None))
                b.wait("vector", (VV, VV.n))
                b.op("vector", "tensor_scalar", sm[:, 2, :], sm[:, 3, :],
                     float(TOPK) - 0.5, None, ALU.is_lt, sem=VV)
                b.wait("vector", (VV, VV.n))
                b.op("vector", "tensor_tensor", cw[:, t, :], sm[:, 1, :],
                     sm[:, 2, :], ALU.mult, sem=VV)
        b.fence(("sync", "tensor", "vector", "scalar"))


def rope_combine(b, psum, prot, qb_sl, rt, cosT, sinT, P_sb, out_ap, elu,
                 toks):
    VV, SS, TT = b.VV, b.SS, b.TT
    (mm_tok, qb_rel, rot_rel, out_rel, rt_prev) = toks
    b.wait("scalar", mm_tok)
    b.waits("scalar", [qb_rel])
    cp = b.op("scalar", "copy", qb_sl, psum[:, :], sem=SS)
    b.wait("tensor", cp)
    b.waits("tensor", [rot_rel])
    rmm = b.mm(prot[:, :], P_sb[:, :], qb_sl, start=True, stop=True, sem=TT)
    b.wait("vector", rmm)
    b.waits("vector", [rt_prev])
    b.op("vector", "tensor_tensor", rt[:, 0, :], qb_sl, cosT, ALU.mult)
    t_rot = b.op("vector", "tensor_tensor", rt[:, 1, :], prot[:, :], sinT,
                 ALU.mult, sem=VV)
    b.wait("vector", (VV, VV.n))
    if not elu:
        b.waits("vector", [out_rel])
        otok = b.op("vector", "tensor_tensor", out_ap, rt[:, 0, :],
                    rt[:, 1, :], ALU.add, sem=VV)
        return cp, t_rot, t_rot, otok
    b.op("vector", "tensor_tensor", rt[:, 2, :], rt[:, 0, :], rt[:, 1, :],
         ALU.add, sem=VV)
    b.wait("vector", (VV, VV.n))
    mn = b.op("vector", "tensor_scalar", rt[:, 3, :], rt[:, 2, :], 0.0, None,
              ALU.min, sem=VV)
    b.wait("scalar", mn)
    ex_ = b.op("scalar", "activation", rt[:, 0, :], rt[:, 3, :], AF.Exp,
               sem=SS)
    b.op("vector", "tensor_scalar", rt[:, 1, :], rt[:, 2, :], 0.0, None,
         ALU.max, sem=VV)
    b.wait("vector", ex_)
    b.wait("vector", (VV, VV.n))
    b.waits("vector", [out_rel])
    otok = b.op("vector", "tensor_tensor", out_ap, rt[:, 0, :], rt[:, 1, :],
                ALU.add, sem=VV)
    return cp, t_rot, t_rot, otok


def attn_a(b):
    nc, io = b.nc, b.io
    TT, VV, SS = b.TT, b.VV, b.SS

    for h2 in range(2):
        with contextlib.ExitStack() as ex:
            xnT = ex.enter_context(b.sbuf("a_xnT", [128, 16, 2048],
                                                  BF16))
            ln_transpose(b, 16,
                         lambda t: io["x0"][h2 * 2048 + t * 128:
                                            h2 * 2048 + (t + 1) * 128, :],
                         xnT)
            wkv = ex.enter_context(b.sbuf("a_w", [128, 2, 16, 512],
                                                  BF16))
            cosh = ex.enter_context(b.sbuf("a_cos", [128, 2048], F32))
            sinh = ex.enter_context(b.sbuf("a_sin", [128, 2048], F32))
            qb = ex.enter_context(b.sbuf("a_qb", [128, 2, 512], BF16))
            rt = ex.enter_context(b.sbuf("a_rt", [128, 4, 512], F32))
            qf = ex.enter_context(b.sbuf("a_qf", [128, 2, 512], BF16))
            pq0 = ex.enter_context(b.psum("a_pq0", [128, 512], F32))
            pq1 = ex.enter_context(b.psum("a_pq1", [128, 512], F32))
            pr0 = ex.enter_context(b.psum("a_pr0", [128, 512], F32))
            pr1 = ex.enter_context(b.psum("a_pr1", [128, 512], F32))
            pq = (pq0, pq1)
            pr = (pr0, pr1)
            b.dma("sync", cosh[:, :],
                  io["cosa"][:, h2 * 2048:(h2 + 1) * 2048], key="csl")
            cstok = b.dma("sync", sinh[:, :],
                          io["sina"][:, h2 * 2048:(h2 + 1) * 2048], key="csl")
            b.wait("vector", cstok)
            w_tok = [None, None]
            w_rel = [None, None]

            def load_w(fg):
                b.wait("gpsimd", w_rel[fg % 2])
                b.wait("gpsimd", b.cast_need("qkv"))
                w_tok[fg % 2] = b.dma(
                    "gpsimd", wkv[:, fg % 2, :, :],
                    io["wqkv_bf"][:, fg * 512:(fg + 1) * 512]
                    .rearrange("(kd p) f -> p kd f", p=128), key=("wA", fg % 2))

            b.gfence()
            load_w(0)
            load_w(1)
            ps_rel = [None, None]
            qb_rel = [None, None]
            rot_rel = [None, None]
            out_rel = [None, None]
            rt_prev = [None]
            it = 0
            for fg in range(12):
                kind = "q" if fg < 4 else ("k" if fg < 8 else "v")
                b.wait("tensor", w_tok[fg % 2])
                last_mm = None
                if kind in ("q", "k"):
                    base = "qT_a" if kind == "q" else "kT_a"
                    for qc in range(4):
                        h = (fg % 4) * 4 + qc
                        for tg in range(4):
                            sl = it % 2
                            b.wait("tensor", ps_rel[sl])
                            mm = None
                            for kd in range(16):
                                mm = b.mm(
                                    pq[sl][:, :],
                                    wkv[:, fg % 2, kd, qc * 128:(qc + 1) * 128],
                                    xnT[:, kd, tg * 512:(tg + 1) * 512],
                                    start=(kd == 0), stop=(kd == 15),
                                    sem=(TT if kd == 15 else None))
                            last_mm = mm
                            cp, t_rot, _, otok = rope_combine(
                                b, pq[sl], pr[sl], qb[:, sl, :], rt,
                                cosh[:, tg * 512:(tg + 1) * 512],
                                sinh[:, tg * 512:(tg + 1) * 512],
                                io["PaT_sb"], qf[:, sl, :], True,
                                (mm, qb_rel[sl], rot_rel[sl], out_rel[sl],
                                 rt_prev[0]))
                            rt_prev[0] = otok
                            ps_rel[sl] = cp
                            qb_rel[sl] = t_rot
                            rot_rel[sl] = t_rot
                            b.wait("sync", otok)
                            out_rel[sl] = b.dma(
                                "sync",
                                io[base][h * 128:(h + 1) * 128,
                                         h2 * 2048 + tg * 512:
                                         h2 * 2048 + (tg + 1) * 512],
                                qf[:, sl, :], key=("spq", sl))
                            it += 1
                else:
                    vg = fg - 8
                    for t in range(16):
                        sl = it % 2
                        b.wait("tensor", ps_rel[sl])
                        mm = None
                        for kd in range(16):
                            mm = b.mm(pq[sl][:, :],
                                      xnT[:, kd, t * 128:(t + 1) * 128],
                                      wkv[:, fg % 2, kd, :],
                                      start=(kd == 0), stop=(kd == 15),
                                      sem=(TT if kd == 15 else None))
                        last_mm = mm
                        b.wait("scalar", mm)
                        b.waits("scalar", [out_rel[sl]])
                        cp = b.op("scalar", "copy", qf[:, sl, :], pq[sl][:, :],
                                  sem=SS)
                        ps_rel[sl] = cp
                        b.wait("sync", cp)
                        out_rel[sl] = b.dma(
                            "sync",
                            io["v_a"][h2 * 2048 + t * 128:
                                      h2 * 2048 + (t + 1) * 128,
                                      vg * 512:(vg + 1) * 512],
                            qf[:, sl, :], key=("spq", sl))
                        it += 1
                if fg < 10:
                    w_rel[fg % 2] = last_mm
                    load_w(fg + 2)
            b.fence()

    with contextlib.ExitStack() as ex:
        qTh = ex.enter_context(b.sbuf("a_qTh", [128, 2, 4096], BF16))
        kTh = ex.enter_context(b.sbuf("a_kTh", [128, 2, 4096], BF16))
        vex = ex.enter_context(b.sbuf("a_vex", [128, 2, 32, 129],
                                              BF16))
        ktm = ex.enter_context(b.sbuf("a_ktm", [128, 2, 2, 128],
                                              BF16))
        Sbuf = ex.enter_context(b.sbuf("a_Sb", [128, 129], BF16))
        attm = ex.enter_context(b.sbuf("a_attm", [128, 2, 512], BF16))
        dcl = ex.enter_context(b.sbuf("a_dcl", [1, 2, 256], F32))
        dbc = ex.enter_context(b.sbuf("a_dbc", [128, 2, 256], F32))
        ctxb = ex.enter_context(b.sbuf("a_ctx", [128, 2, 256], BF16))
        pa = ex.enter_context(b.psum("a_pa", [128, 512], F32))
        pn0 = ex.enter_context(b.psum("a_pn0", [128, 512], F32))
        pn1 = ex.enter_context(b.psum("a_pn1", [128, 512], F32))
        pS = ex.enter_context(b.psum("a_pS", [128, 129], F32))
        pb0 = ex.enter_context(b.psum("a_pb0", [128, 256], F32))
        pb1 = ex.enter_context(b.psum("a_pb1", [128, 256], F32))
        ptk = ex.enter_context(b.psum("a_ptk", [128, 256], BF16))
        pn = (pn0, pn1)
        pb = (pb0, pb1)
        otok = b.op("vector", "memset", vex[:, :, :, 128:129], 1.0, sem=VV)
        b.wait("sync", otok)
        head_rel = [None, None]
        mask_tok = None
        ktok_prev = None
        ktm_prev = [None, None]
        nd_prev = [None, None]
        dma_prev = [None, None]
        dcl_tok = [None, None]
        dbc_tok = [None, None]
        ctx_tok = [None, None]
        for h in range(HA):
            sl = h % 2
            b.waits("sync", [head_rel[sl]])
            qld = b.dma("sync", qTh[:, sl, :],
                        io["qT_a"][h * 128:(h + 1) * 128, :], key=("sq", sl))
            kld = b.dma("sync", kTh[:, sl, :],
                        io["kT_a"][h * 128:(h + 1) * 128, :], key=("sk", sl))
            vld = b.dma("sync", vex[:, sl, :, 0:128],
                        io["v_a"][:, h * 128:(h + 1) * 128]
                        .rearrange("(tt p) d -> p tt d", p=128),
                        key=("sv", sl))
            b.waits("tensor", [qld, kld, vld])
            nd_tok = {}
            sc_tok = None
            for c in range(16):
                cp_ = c % 2
                b.waits("tensor", [ktok_prev])
                tk = None
                for j in range(2):
                    tk = b.tr(ptk[:, j * 128:(j + 1) * 128],
                              kTh[:, sl, c * 256 + j * 128:
                                  c * 256 + (j + 1) * 128],
                              sem=(TT if j == 1 else None))
                b.wait("scalar", tk)
                b.waits("scalar", [ktm_prev[cp_]])
                ktok = b.op("scalar", "copy", ktm[:, cp_, :, :],
                            ptk[:, :].rearrange("p (j t) -> p j t", t=128),
                            sem=SS)
                ktok_prev = ktok
                if c > 0:
                    b.wait("scalar", sc_tok)
                    stok = b.op("scalar", "copy", Sbuf[:, :], pS[:, :], sem=SS)
                else:
                    stok = b.op("vector", "memset", Sbuf[:, :], 0.0, sem=VV)
                b.wait("tensor", mask_tok)
                am = None
                for kt in range(2):
                    am = b.mm(pa[:, kt * 256:(kt + 1) * 256],
                              kTh[:, sl, c * 256 + kt * 128:
                                  c * 256 + kt * 128 + 128],
                              qTh[:, sl, c * 256:(c + 1) * 256],
                              start=(kt == 0), stop=(kt == 1),
                              sem=(TT if kt == 1 else None))
                b.wait("vector", am)
                b.waits("vector", [nd_prev[cp_]])
                mask_tok = b.op("vector", "tensor_tensor", attm[:, cp_, :],
                                pa[:, :], io["trilTa_sb"][:, :], ALU.mult,
                                sem=VV)
                b.waits("tensor", [mask_tok, ktok, stok, ctx_tok[cp_]])
                pnb = pn[cp_]
                for kt in range(2):
                    b.mm(pnb[:, 0:256], vex[:, sl, 2 * c + kt, 0:128],
                         attm[:, cp_, kt * 256:(kt + 1) * 256],
                         start=(kt == 0), stop=False)
                    b.mm(pnb[0:1, 256:512], io["ones_col_sb"][:, :],
                         attm[:, cp_, kt * 256:(kt + 1) * 256],
                         start=False, stop=False)
                b.mm(pnb[:, 0:256], Sbuf[:, 0:128],
                     qTh[:, sl, c * 256:(c + 1) * 256], start=False,
                     stop=False)
                nd_tok[c] = b.mm(pnb[0:1, 256:512], Sbuf[:, 128:129],
                                 qTh[:, sl, c * 256:(c + 1) * 256],
                                 start=False, stop=True, sem=TT)
                nd_prev[cp_] = nd_tok[c]
                for j in range(2):
                    sc_tok = b.mm(pS[:, :], ktm[:, cp_, j, :],
                                  vex[:, sl, 2 * c + j, :],
                                  start=(c == 0 and j == 0), stop=False,
                                  sem=(TT if j == 1 else None))
                ktm_prev[cp_] = sc_tok
                b.wait("vector", nd_tok[c])
                b.waits("vector", [dcl_tok[cp_]])
                b.op("vector", "tensor_scalar", dcl[:, cp_, :],
                     pnb[0:1, 256:512], 1e-6, None, ALU.max, sem=VV)
                b.wait("vector", (VV, VV.n))
                rc = b.op("vector", "reciprocal", dcl[:, cp_, :],
                          dcl[:, cp_, :], sem=VV)
                b.wait("tensor", rc)
                b.waits("tensor", [dbc_tok[cp_]])
                bc = b.mm(pb[cp_][:, :], io["ones_row_sb"][0:1, :],
                          dcl[:, cp_, :], start=True, stop=True, sem=TT)
                dcl_tok[cp_] = bc
                b.wait("scalar", bc)
                b.waits("scalar", [ctx_tok[cp_]])
                dbt = b.op("scalar", "copy", dbc[:, cp_, :], pb[cp_][:, :],
                           sem=SS)
                dbc_tok[cp_] = dbt
                b.wait("vector", dbt)
                b.waits("vector", [dma_prev[cp_]])
                ct = b.op("vector", "tensor_tensor", ctxb[:, cp_, :],
                          pn[cp_][:, 0:256], dbc[:, cp_, :], ALU.mult, sem=VV)
                ctx_tok[cp_] = ct
                b.wait("sync", ct)
                dma_prev[cp_] = b.dma(
                    "sync",
                    io["ctxA"][h * 128:(h + 1) * 128, c * 256:(c + 1) * 256],
                    ctxb[:, cp_, :], key=("ca", cp_))
                if c == 15:
                    head_rel[sl] = dma_prev[cp_]
        b.fence()

    outproj(b, "wout_bf", "ctxA", io["x0"], io["x1"], 32,
            cast_tok=b.cast_need("wout"))


def outproj(b, wname, ctxname, xres, dst, ntiles, cast_tok, gather_idx=None):
    nc, io = b.nc, b.io
    TT, VV, SS = b.TT, b.VV, b.SS
    with contextlib.ExitStack() as ex:
        wb = ex.enter_context(b.sbuf("op_w", [128, 2, 16, 512], BF16))
        ct = ex.enter_context(b.sbuf("op_ct", [128, 2, 16, 128], BF16))
        xb = ex.enter_context(b.sbuf("op_x", [128, 2, 512], F32))
        ob = ex.enter_context(b.sbuf("op_o", [128, 2, 512], F32))
        pp0 = ex.enter_context(b.psum("op_p0", [128, 512], F32))
        pp1 = ex.enter_context(b.psum("op_p1", [128, 512], F32))
        pp = (pp0, pp1)
        w_tok = [None, None]
        w_rel = [None, None]
        ct_rel = [None, None]
        x_rel = [None, None]
        o_rel = [None, None]
        ev = [None, None]

        def load_w(og):
            b.wait("gpsimd", w_rel[og % 2])
            b.wait("gpsimd", cast_tok)
            w_tok[og % 2] = b.dma(
                "gpsimd", wb[:, og % 2, :, :],
                io[wname][:, og * 512:(og + 1) * 512]
                .rearrange("(rc p) f -> p rc f", p=128), key=("wO", og % 2))

        b.gfence()
        load_w(0)
        load_w(1)
        it = 0
        for og in range(4):
            b.wait("tensor", w_tok[og % 2])
            last_mm = None
            for t in range(ntiles):
                sl = it % 2
                b.wait("sync", ct_rel[sl])
                cld = b.dma("sync", ct[:, sl, :, :],
                            io[ctxname][:, t * 128:(t + 1) * 128]
                            .rearrange("(rc p) t -> p rc t", p=128),
                            key=("oct", sl))
                if gather_idx is not None:
                    b.wait("gpsimd", x_rel[sl])
                    xld = b.idma("gpsimd", xb[:, sl, :],
                                 xres[0:S, 0:512],
                                 gather_idx[:, t:t + 1], key=("oxg", sl),
                                 eoff=og * 512)
                else:
                    b.wait("sync", x_rel[sl])
                    xld = b.dma("sync", xb[:, sl, :],
                                xres[t * 128:(t + 1) * 128,
                                     og * 512:(og + 1) * 512], key=("ox", sl))
                b.waits("tensor", [cld, ev[sl]])
                mm = None
                for rc in range(16):
                    mm = b.mm(pp[sl][:, :], ct[:, sl, rc, :],
                              wb[:, og % 2, rc, :], start=(rc == 0),
                              stop=(rc == 15), sem=(TT if rc == 15 else None))
                ct_rel[sl] = last_mm = mm
                b.waits("vector", [mm, xld, o_rel[sl]])
                x_rel[sl] = ev[sl] = b.op(
                    "vector", "tensor_tensor", ob[:, sl, :], pp[sl][:, :],
                    xb[:, sl, :], ALU.add, sem=VV)
                b.wait("sync", ev[sl])
                o_rel[sl] = b.dma(
                    "sync", dst[t * 128:(t + 1) * 128,
                                og * 512:(og + 1) * 512],
                    ob[:, sl, :], key=("oy", sl))
                it += 1
            if og < 2:
                w_rel[og % 2] = last_mm
                load_w(og + 2)
        b.fence()


def moe(b, xsrc, dst, ntok, gather_idx=None):
    nc, io = b.nc, b.io
    TT, VV, SS = b.TT, b.VV, b.SS
    ngrp = ntok // 1024
    if MOE_NGRP is not None:
        ngrp = min(ngrp, MOE_NGRP)
    with contextlib.ExitStack() as ex:
        cw = ex.enter_context(b.sbuf("m_cw", [128, 8, 16], F32))
        xnT = ex.enter_context(b.sbuf("m_xnT", [128, 16, 1024], BF16))
        y = ex.enter_context(b.sbuf("m_y", [128, 8, 2048], F32))
        aT = ex.enter_context(b.sbuf("m_aT", [128, 11, 1024], BF16))
        wd = ex.enter_context(b.sbuf("m_wd", [128, 11, 1024], BF16))
        wg = ex.enter_context(b.sbuf("m_wg", [128, 2, 16, 128], BF16))
        wu = ex.enter_context(b.sbuf("m_wu", [128, 2, 16, 128], BF16))
        tmp = ex.enter_context(b.sbuf("m_tmp", [128, 2, 512], F32))
        pg0 = ex.enter_context(b.psum("m_pg0", [128, 512], F32))
        pg1 = ex.enter_context(b.psum("m_pg1", [128, 512], F32))
        pu0 = ex.enter_context(b.psum("m_pu0", [128, 512], F32))
        pu1 = ex.enter_context(b.psum("m_pu1", [128, 512], F32))
        pgb = (pg0, pg1)
        pub = (pu0, pu1)
        for g in range(ngrp):
            tb = g * 8

            def src_rows(t, tb=tb):
                if gather_idx is not None:
                    return ("gather", xsrc[0:128, :],
                            gather_idx[:, tb + t:tb + t + 1])
                return xsrc[(tb + t) * 128:(tb + t + 1) * 128, :]

            ln_transpose(b, 8, src_rows, xnT, gating=True, cw=cw)
            yld = None
            for t in range(8):
                if gather_idx is not None:
                    yld = b.idma("gpsimd", y[:, t, :], xsrc[0:S, :],
                                 gather_idx[:, tb + t:tb + t + 1], key="yinit")
                else:
                    yld = b.dma("sync", y[:, t, :],
                                xsrc[(tb + t) * 128:(tb + t + 1) * 128, :],
                                key="yinit")
            b.wait("vector", yld)
            with contextlib.ExitStack() as ex2:
                pd0 = ex2.enter_context(b.psum("m_pd0", [128, 512],
                                                       F32))
                pd1 = ex2.enter_context(b.psum("m_pd1", [128, 512],
                                                       F32))
                pdb = (pd0, pd1)
                wg_tok = [None, None]
                wg_rel = [None, None]
                wd_rel = None
                aT_free = None
                tmp_rel = [None, None]
                sv_last = [None, None]
                gu_done = {}

                def load_gu(e, mc):
                    sl = (e * NMC + mc) % 2
                    b.wait("gpsimd", wg_rel[sl])
                    b.wait("gpsimd", b.cast_need("expert", e))
                    b.dma("gpsimd", wg[:, sl, :, :],
                          io["wg_bf"][e, :, mc * 128:(mc + 1) * 128]
                          .rearrange("(kd p) m -> p kd m", p=128),
                          key=("mwg", sl))
                    wg_tok[sl] = b.dma(
                        "gpsimd", wu[:, sl, :, :],
                        io["wu_bf"][e, :, mc * 128:(mc + 1) * 128]
                        .rearrange("(kd p) m -> p kd m", p=128),
                        key=("mwg", sl))

                def load_wd(e, dh):
                    b.wait("sync", wd_rel)
                    b.wait("sync", b.cast_need("expert", e))
                    return b.dma("sync", wd[:, :, :],
                                 io["wd_bf"][e][:, dh * 1024:(dh + 1) * 1024]
                                 .rearrange("(mc p) d -> p mc d", p=128),
                                 key="mwd")

                NEL = MOE_NE
                if NEL > 0:
                    b.gfence()
                    load_gu(0, 0)
                    load_gu(0, 1)
                    wd_tok = load_wd(0, 0)
                for e in range(NEL):
                    for mc in range(NMC):
                        sl = (e * NMC + mc) % 2
                        b.wait("tensor", wg_tok[sl])
                        for tg in range(2):
                            b.waits("tensor", [sv_last[tg]])
                            gm = um = None
                            for kd in range(16):
                                gm = b.mm(pgb[tg][:, :], wg[:, sl, kd, :],
                                          xnT[:, kd, tg * 512:(tg + 1) * 512],
                                          start=(kd == 0), stop=(kd == 15),
                                          sem=(TT if kd == 15 else None))
                            b.waits("tensor", [gu_done.get((e, mc - 1, tg)),
                                               gu_done.get((e - 1, NMC - 1,
                                                            tg))
                                               if mc == 0 else None])
                            for kd in range(16):
                                um = b.mm(pub[tg][:, :], wu[:, sl, kd, :],
                                          xnT[:, kd, tg * 512:(tg + 1) * 512],
                                          start=(kd == 0), stop=(kd == 15),
                                          sem=(TT if kd == 15 else None))
                            if tg == 1:
                                wg_rel[sl] = um
                                if mc + 2 < NMC:
                                    load_gu(e, mc + 2)
                                elif e + 1 < NEL:
                                    load_gu(e + 1, mc + 2 - NMC)
                            b.wait("scalar", gm)
                            b.waits("scalar", [tmp_rel[tg]])
                            sv = b.op("scalar", "activation", tmp[:, tg, :],
                                      pgb[tg][:, :], AF.Silu, sem=SS)
                            sv_last[tg] = sv
                            b.waits("vector", [sv, um, aT_free])
                            gu_done[(e, mc, tg)] = b.op(
                                "vector", "tensor_tensor",
                                aT[:, mc, tg * 512:(tg + 1) * 512],
                                tmp[:, tg, :], pub[tg][:, :], ALU.mult,
                                sem=VV)
                            tmp_rel[tg] = gu_done[(e, mc, tg)]
                    b.waits("tensor", [gu_done[(e, NMC - 1, 0)],
                                       gu_done[(e, NMC - 1, 1)]])
                    dn_ev = []
                    last_dn = None
                    i = 0
                    for dh in range(2):
                        b.wait("tensor", wd_tok)
                        for tt in range(8):
                            for dch in range(2):
                                sl = i % 2
                                if i >= 2:
                                    b.waits("tensor", [dn_ev[i - 2]])
                                dm = None
                                for mc in range(NMC):
                                    dm = b.mm(
                                        pdb[sl][:, :],
                                        aT[:, mc, tt * 128:(tt + 1) * 128],
                                        wd[:, mc, dch * 512:(dch + 1) * 512],
                                        start=(mc == 0),
                                        stop=(mc == NMC - 1),
                                        sem=(TT if mc == NMC - 1 else None))
                                last_dn = dm
                                b.wait("vector", dm)
                                col = dh * 1024 + dch * 512
                                dn_ev.append(b.op(
                                    "vector", "scalar_tensor_tensor",
                                    y[:, tt, col:col + 512],
                                    pdb[sl][:, :], cw[:, tt, e:e + 1],
                                    y[:, tt, col:col + 512],
                                    ALU.mult, ALU.add, sem=VV))
                                i += 1
                        wd_rel = last_dn
                        if dh == 0:
                            wd_tok = load_wd(e, 1)
                        elif e + 1 < NEL:
                            wd_tok = load_wd(e + 1, 0)
                    aT_free = last_dn
                b.fence(("sync", "tensor", "vector", "scalar"))
            for t in range(8):
                for og in range(4):
                    b.dma("sync", dst[(tb + t) * 128:(tb + t + 1) * 128,
                                      og * 512:(og + 1) * 512],
                          y[:, t, og * 512:(og + 1) * 512],
                          key=("myo", (t * 4 + og) % 2))
            b.fence(("sync",))
        b.fence()


def attn_b(b):
    nc, io = b.nc, b.io
    TT, VV, SS = b.TT, b.VV, b.SS
    with contextlib.ExitStack() as ex0:
        stg = ex0.enter_context(b.sbuf("b_g", [128, 2, D], F32))
        b.gfence()
        g_rel = [None, None]
        for t in range(16):
            sl = t % 2
            b.waits("gpsimd", [g_rel[sl]])
            gld = b.idma("gpsimd", stg[:, sl, :], io["x2"][0:S, :],
                         io["idx_sb"][:, t:t + 1], key=("x2g", sl))
            b.wait("sync", gld)
            g_rel[sl] = b.dma("sync", io["x2own"][t * 128:(t + 1) * 128, :],
                              stg[:, sl, :], key=("x2w", sl))
        b.fence(("sync",))
        b.gfence()
    with contextlib.ExitStack() as exA:
        kTb = exA.enter_context(b.sbuf("b_kT", [128, 4, 4096], BF16))
        vexb = exA.enter_context(b.sbuf("b_vex", [128, 32, 520],
                                                BF16))
        ones_tok = b.op(
            "vector", "memset",
            vexb[:, :, :].rearrange("p kt (g c) -> p kt g c", c=65)
            [:, :, :, 64:65], 1.0, sem=VV)

        with contextlib.ExitStack() as ex:
            xnT = ex.enter_context(b.sbuf("b_xnT", [128, 16, 1024],
                                                  BF16))
            wkb = ex.enter_context(b.sbuf("b_wk", [128, 16, 512],
                                                  BF16))
            wvb = ex.enter_context(b.sbuf("b_wv", [128, 16, 512],
                                                  BF16))
            cosf = ex.enter_context(b.sbuf("b_cos", [128, 4096], F32))
            sinf = ex.enter_context(b.sbuf("b_sin", [128, 4096], F32))
            qb = ex.enter_context(b.sbuf("b_qb", [128, 2, 512], BF16))
            rt = ex.enter_context(b.sbuf("b_rt", [128, 2, 512], F32))
            pk0 = ex.enter_context(b.psum("b_pk0", [128, 512], F32))
            pk1 = ex.enter_context(b.psum("b_pk1", [128, 512], F32))
            pr0 = ex.enter_context(b.psum("b_pr0", [128, 512], F32))
            pr1 = ex.enter_context(b.psum("b_pr1", [128, 512], F32))
            pk = (pk0, pk1)
            prr = (pr0, pr1)
            b.dma("sync", cosf[:, :], io["cosb"], key="csl")
            cstok = b.dma("sync", sinf[:, :], io["sinb"], key="csl")
            b.wait("vector", cstok)
            b.wait("gpsimd", b.cast_need("attnb"))
            wk_t = b.dma("gpsimd", wkb[:, :, :],
                         io["wk_bf"].rearrange("(kd p) r -> p kd r", p=128),
                         key="bwk")
            wv_t = b.dma("gpsimd", wvb[:, :, :],
                         io["wv_bf"].rearrange("(kd p) r -> p kd r", p=128),
                         key="bwk")
            for g in range(4):
                ln_transpose(b, 8,
                             lambda t, g=g: io["x2"][(g * 8 + t) * 128:
                                                     (g * 8 + t + 1) * 128, :],
                             xnT)
                b.waits("tensor", [wk_t, wv_t])
                ps_rel = [None, None]
                qb_rel = [None, None]
                rot_rel = [None, None]
                out_rel = [None, None]
                rt_prev = [None]
                it = 0
                for kc in range(4):
                    for tg in range(2):
                        sl = it % 2
                        b.wait("tensor", ps_rel[sl])
                        mm = None
                        for kd in range(16):
                            mm = b.mm(pk[sl][:, :],
                                      wkb[:, kd, kc * 128:(kc + 1) * 128],
                                      xnT[:, kd, tg * 512:(tg + 1) * 512],
                                      start=(kd == 0), stop=(kd == 15),
                                      sem=(TT if kd == 15 else None))
                        col = g * 1024 + tg * 512
                        cp, t_rot, _, otok = rope_combine(
                            b, pk[sl], prr[sl], qb[:, sl, :], rt,
                            cosf[:, col:col + 512], sinf[:, col:col + 512],
                            io["PbT_sb"], kTb[:, kc, col:col + 512], False,
                            (mm, qb_rel[sl], rot_rel[sl], out_rel[sl],
                             rt_prev[0]))
                        rt_prev[0] = otok
                        ps_rel[sl] = cp
                        qb_rel[sl] = t_rot
                        rot_rel[sl] = t_rot
                        out_rel[sl] = otok
                        it += 1
                for t in range(8):
                    sl = it % 2
                    b.wait("tensor", ps_rel[sl])
                    mm = None
                    for kd in range(16):
                        mm = b.mm(pk[sl][:, :],
                                  xnT[:, kd, t * 128:(t + 1) * 128],
                                  wvb[:, kd, :], start=(kd == 0),
                                  stop=(kd == 15),
                                  sem=(TT if kd == 15 else None))
                    b.wait("vector", mm)
                    b.wait("vector", ones_tok)
                    ps_rel[sl] = b.op(
                        "vector", "tensor_copy",
                        vexb[:, g * 8 + t, :]
                        .rearrange("p (gg c) -> p gg c", c=65)[:, :, 0:64],
                        pk[sl][:, :].rearrange("p (gg c) -> p gg c", c=64),
                        sem=VV)
                    it += 1
                b.fence(("sync", "tensor", "vector", "scalar"))

        with contextlib.ExitStack() as exQ:
            qct = exQ.enter_context(b.sbuf("b_qct", [128, 16, 2048],
                                                   BF16))
            with contextlib.ExitStack() as ex:
                xnT = ex.enter_context(b.sbuf("b_xnTq",
                                                      [128, 16, 1024], BF16))
                wqb = ex.enter_context(b.sbuf("b_wq",
                                                      [128, 2, 16, 128], BF16))
                cosq = ex.enter_context(b.sbuf("b_cosq", [128, 1024], F32))
                sinq = ex.enter_context(b.sbuf("b_sinq", [128, 1024], F32))
                qb = ex.enter_context(b.sbuf("b_qb2", [128, 2, 512],
                                                     BF16))
                rt = ex.enter_context(b.sbuf("b_rt2", [128, 2, 512], F32))
                pk0 = ex.enter_context(b.psum("b_pq0", [128, 512],
                                                      F32))
                pk1 = ex.enter_context(b.psum("b_pq1", [128, 512],
                                                      F32))
                pr0 = ex.enter_context(b.psum("b_pr2", [128, 512],
                                                      F32))
                pr1 = ex.enter_context(b.psum("b_pr3", [128, 512],
                                                      F32))
                pk = (pk0, pk1)
                prr = (pr0, pr1)
                wq_tok = [None, None]
                wq_rel = [None, None]

                def load_wq(qc):
                    sl = qc % 2
                    b.wait("gpsimd", wq_rel[sl])
                    wq_tok[sl] = b.dma(
                        "gpsimd", wqb[:, sl, :, :],
                        io["wq_bf"][:, qc * 128:(qc + 1) * 128]
                        .rearrange("(kd p) r -> p kd r", p=128),
                        key=("bwq", sl))

                for hf in range(2):
                    b.fence(("sync",))
                    b.gfence()
                    b.dma("sync", cosq[:, :],
                          io["cosbq"][:, hf * 1024:(hf + 1) * 1024], key="csl")
                    cstok = b.dma("sync", sinq[:, :],
                                  io["sinbq"][:, hf * 1024:(hf + 1) * 1024],
                                  key="csl")
                    b.wait("vector", cstok)
                    ln_transpose(
                        b, 8,
                        lambda t, hf=hf: io["x2own"][(hf * 8 + t) * 128:
                                                     (hf * 8 + t + 1) * 128,
                                                     :],
                        xnT)
                    if hf == 0:
                        load_wq(0)
                        load_wq(1)
                    ps_rel = [None, None]
                    qb_rel = [None, None]
                    rot_rel = [None, None]
                    out_rel = [None, None]
                    rt_prev = [None]
                    it = 0
                    for qc in range(16):
                        b.wait("tensor", wq_tok[qc % 2])
                        last_mm = None
                        for tg in range(2):
                            sl = it % 2
                            b.wait("tensor", ps_rel[sl])
                            mm = None
                            for kd in range(16):
                                mm = b.mm(pk[sl][:, :], wqb[:, qc % 2, kd, :],
                                          xnT[:, kd, tg * 512:(tg + 1) * 512],
                                          start=(kd == 0), stop=(kd == 15),
                                          sem=(TT if kd == 15 else None))
                            last_mm = mm
                            col = hf * 1024 + tg * 512
                            cp, t_rot, _, otok = rope_combine(
                                b, pk[sl], prr[sl], qb[:, sl, :], rt,
                                cosq[:, tg * 512:(tg + 1) * 512],
                                sinq[:, tg * 512:(tg + 1) * 512],
                                io["PbT_sb"], qct[:, qc, col:col + 512], False,
                                (mm, qb_rel[sl], rot_rel[sl], out_rel[sl],
                                 rt_prev[0]))
                            rt_prev[0] = otok
                            ps_rel[sl] = cp
                            qb_rel[sl] = t_rot
                            rot_rel[sl] = t_rot
                            out_rel[sl] = otok
                            it += 1
                        wq_rel[qc % 2] = last_mm
                        if qc < 14:
                            load_wq(qc + 2)
                        elif hf == 0:
                            load_wq(qc + 2 - 16)
                    b.fence(("sync", "tensor", "vector", "scalar"))

            with contextlib.ExitStack() as ex:
                maskb = ex.enter_context(b.sbuf("b_mask",
                                                        [128, 6656], BF16))
                esc = ex.enter_context(b.sbuf("b_esc", [128, 2, 512],
                                                      F32))
                atm = ex.enter_context(b.sbuf("b_atm", [128, 2, 512],
                                                      BF16))
                dbc = ex.enter_context(b.sbuf("b_dbc", [64, 2, 512],
                                                      F32))
                rec = ex.enter_context(b.sbuf("b_rec", [128, 2, 512],
                                                      F32))
                recl = ex.enter_context(b.sbuf("b_recl", [1, 2, 512], F32))
                cstg = ex.enter_context(b.sbuf("b_cst",
                                                       [128, 4, 512], BF16))
                ps0 = ex.enter_context(b.psum("b_ps0", [128, 512],
                                                      F32))
                ps1 = ex.enter_context(b.psum("b_ps1", [128, 512],
                                                      F32))
                pc0 = ex.enter_context(b.psum("b_pc0", [128, 512],
                                                      F32))
                pc1 = ex.enter_context(b.psum("b_pc1", [128, 512],
                                                      F32))
                pb0 = ex.enter_context(b.psum("b_pbc", [128, 512],
                                                      F32))
                psb = (ps0, ps1)
                pcb = (pc0, pc1)
                mtok = b.dma("sync", maskb[:, :], io["maskb"], key="csl")
                b.wait("vector", mtok)
                stg_tok = {}
                pcb_rel = [None, None]
                rec_rel = [None, None]
                recl_rel = [None, None]
                dbc_rel = [None, None]
                pb_rel = None
                for hc in range(ATTNB_NHC):
                    qc, o01 = hc // 2, hc % 2
                    po = o01 * 64
                    g = (qc // 4) * 2 + o01
                    kvc, kvo = qc // 4, o01 * 64
                    for qg in range(4):
                        qp = qg % 2
                        exp_tok = [None, None]
                        atm_tok = [None, None]
                        cm = None
                        for kt in range(32):
                            sl = kt % 2
                            b.waits("tensor", [exp_tok[sl]])
                            if kt == 0:
                                b.waits("tensor", [pcb_rel[qp]])
                            sm_ = b.mm(
                                psb[sl][:, :],
                                kTb[kvo:kvo + 64, kvc,
                                    kt * 128:(kt + 1) * 128],
                                qct[po:po + 64, qc, qg * 512:(qg + 1) * 512],
                                start=True, stop=True, sem=TT)
                            b.wait("scalar", sm_)
                            b.waits("scalar", [atm_tok[sl]])
                            et = b.op("scalar", "activation", esc[:, sl, :],
                                      psb[sl][:, :], AF.Exp, scale=0.125,
                                      bias=io["c_m8"], sem=SS)
                            exp_tok[sl] = et
                            b.wait("vector", et)
                            mo = qg * 512 + 4096 - kt * 128
                            at = b.op("vector", "tensor_tensor",
                                      atm[:, sl, :], esc[:, sl, :],
                                      maskb[:, mo:mo + 512], ALU.mult, sem=VV)
                            b.wait("tensor", at)
                            cm = b.mm(pcb[qp][0:65, :],
                                      vexb[:, kt, g * 65:(g + 1) * 65],
                                      atm[:, sl, :], start=(kt == 0),
                                      stop=(kt == 31),
                                      sem=(TT if kt == 31 else None))
                            atm_tok[sl] = cm
                        b.wait("vector", cm)
                        b.waits("vector", [rec_rel[qp]])
                        b.op("vector", "tensor_scalar", rec[64:65, qp, :],
                             pcb[qp][64:65, :], 1e-30, None, ALU.max, sem=VV)
                        b.wait("vector", (VV, VV.n))
                        rc = b.op("vector", "reciprocal", rec[64:65, qp, :],
                                  rec[64:65, qp, :], sem=VV)
                        b.wait("sync", rc)
                        b.waits("sync", [recl_rel[qp]])
                        rld = b.dma("sync", recl[0:1, qp, :],
                                    rec[64:65, qp, :], key=("brl", qp))
                        rec_rel[qp] = rld
                        b.wait("tensor", rld)
                        b.waits("tensor", [pb_rel])
                        bc = b.mm(pb0[0:64, :], io["ones_row_sb"][0:1, 0:64],
                                  recl[0:1, qp, :], start=True, stop=True,
                                  sem=TT)
                        recl_rel[qp] = bc
                        b.wait("scalar", bc)
                        b.waits("scalar", [dbc_rel[qp]])
                        db = b.op("scalar", "copy", dbc[:, qp, :],
                                  pb0[0:64, :], sem=SS)
                        pb_rel = db
                        b.wait("vector", db)
                        ct = b.op("vector", "tensor_tensor",
                                  cstg[po:po + 64, qg, :], pcb[qp][0:64, :],
                                  dbc[:, qp, :], ALU.mult, sem=VV)
                        stg_tok[(hc, qg)] = ct
                        pcb_rel[qp] = ct
                        dbc_rel[qp] = ct
                    if hc % 2 == 1:
                        b.waits("vector",
                                [stg_tok[(hc, q_)] for q_ in range(4)]
                                + [stg_tok[(hc - 1, q_)] for q_ in range(4)])
                        for qg in range(4):
                            b.op("vector", "tensor_copy",
                                 qct[:, qc, qg * 512:(qg + 1) * 512],
                                 cstg[:, qg, :], sem=VV)
                b.fence(("sync", "tensor", "vector", "scalar"))

            for rc in range(16):
                b.dma("sync", io["ctxB"][rc * 128:(rc + 1) * 128, :],
                      qct[:, rc, :], key=("spcb", rc % 2))
            b.fence(("sync",))
        outproj(b, "wo_bf", "ctxB", io["x2own"], io["x3own"], 16,
                cast_tok=b.cast_need("attnb"))


def build_nc():
    nc = bass.Bass("TRN2", target_bir_lowering=False)
    b = Bld(nc)
    io = b.io

    def inp(name, shape, dtype):
        io[name] = nc.dram_tensor(name, shape, dtype,
                                  kind="ExternalInput").ap()

    inp("x0", [S, D], F32)
    inp("wqkvT", [D, 3 * D], F32)
    inp("woutT", [D, D], F32)
    inp("wq", [D, D], F32)
    inp("wk", [D, KVH * DB], F32)
    inp("wv", [D, KVH * DB], F32)
    inp("wo", [D, D], F32)
    inp("gatewT", [D, NE], F32)
    inp("wgT", [NE, D, MD], F32)
    inp("wuT", [NE, D, MD], F32)
    inp("wdT", [NE, MD, D], F32)
    inp("ident", [128, 128], BF16)
    inp("PaT", [128, 128], BF16)
    inp("PbT", [128, 128], BF16)
    inp("cosa", [128, S], F32)
    inp("sina", [128, S], F32)
    inp("cosb", [128, S], F32)
    inp("sinb", [128, S], F32)
    inp("cosbq", [128, 2048], F32)
    inp("sinbq", [128, 2048], F32)
    inp("trilTa", [128, 512], BF16)
    inp("maskb", [128, 6656], BF16)
    inp("idx", [128, 16], I32)
    inp("ones_col", [128, 1], BF16)
    inp("ones_row", [1, 128], F32)

    y = nc.dram_tensor("y", [2048, D], F32, kind="ExternalOutput").ap()
    io["y"] = y

    def scratch(name, shape, dtype):
        kind = "ExternalOutput" if name in DBG else "Internal"
        if name in DBG_IN:
            kind = "ExternalInput"
        io[name] = nc.dram_tensor(name, shape, dtype, kind=kind).ap()

    scratch("x1", [S, D], F32)
    scratch("x2", [S, D], F32)
    scratch("x2own", [2048, D], F32)
    scratch("x3own", [2048, D], F32)
    scratch("qT_a", [D, S], BF16)
    scratch("kT_a", [D, S], BF16)
    scratch("v_a", [S, D], BF16)
    scratch("ctxA", [D, S], BF16)
    scratch("ctxB", [D, 2048], BF16)
    scratch("wqkv_bf", [D, 3 * D], BF16)
    scratch("wout_bf", [D, D], BF16)
    scratch("wq_bf", [D, D], BF16)
    scratch("wk_bf", [D, KVH * DB], BF16)
    scratch("wv_bf", [D, KVH * DB], BF16)
    scratch("wo_bf", [D, D], BF16)
    scratch("wg_bf", [NE, D, MD], BF16)
    scratch("wu_bf", [NE, D, MD], BF16)
    scratch("wd_bf", [NE, MD, D], BF16)

    small = [("ident", [128, 128], BF16), ("PaT", [128, 128], BF16),
             ("PbT", [128, 128], BF16), ("trilTa", [128, 512], BF16),
             ("idx", [128, 16], I32), ("ones_col", [128, 1], BF16),
             ("ones_row", [1, 128], F32)]
    ctok = None
    for name, shape, dt in small:
        sb = nc.alloc_sbuf_tensor(name + "_sb", shape, dt)
        apfull = sb[tuple(slice(None) for _ in shape)]
        ctok = b.dma("sync", apfull, io[name], key="consts")
        io[name + "_sb"] = apfull
    gw = nc.alloc_sbuf_tensor("gw_sb", [128, 16, 16], BF16)
    gwtok = b.dma("gpsimd", gw[:, :, :],
                  io["gatewT"].rearrange("(kd p) e -> p kd e", p=128),
                  key="gconsts")
    io["gw_sb"] = gw[:, :, :]
    cns = nc.alloc_sbuf_tensor("cns", [128, 2], F32)
    b.op("vector", "memset", cns[:, 0:1], EPS, sem=b.VV)
    b.op("vector", "memset", cns[:, 1:2], -8.0, sem=b.VV)
    io["c_eps"] = cns[:, 0:1]
    io["c_m8"] = cns[:, 1:2]

    castsems = {}

    def cast(dst_ap, src_ap, grp):
        if grp not in castsems:
            castsems[grp] = Sem(nc, f"cast_{grp}")
        sem = castsems[grp]
        sem.n += 16
        b._add("gpsimd", lambda h, o=dst_ap, i=src_ap, s=sem.h:
               h.dma_start(out=o, in_=i).then_inc(s, 16))

    cast(io["wqkv_bf"], io["wqkvT"], "qkv")
    cast(io["wout_bf"], io["woutT"], "wout")
    for e in range(NE):
        cast(io["wg_bf"][e], io["wgT"][e], f"e{e}")
        cast(io["wu_bf"][e], io["wuT"][e], f"e{e}")
        cast(io["wd_bf"][e], io["wdT"][e], f"e{e}")
    cast(io["wq_bf"], io["wq"], "attnb")
    cast(io["wk_bf"], io["wk"], "attnb")
    cast(io["wv_bf"], io["wv"], "attnb")
    cast(io["wo_bf"], io["wo"], "attnb")

    def cast_need(kind, e=0):
        if kind == "expert":
            sem = castsems[f"e{e}"]
        else:
            sem = castsems[kind]
        return (sem, sem.n)
    b.cast_need = cast_need

    b.wait("vector", ctok)
    b.wait("vector", gwtok)
    b.fence(("sync", "tensor", "vector", "scalar"))

    if N_PHASE >= 1:
        attn_a(b)
    if N_PHASE >= 2:
        moe(b, io["x1"], io["x2"], S)
    if N_PHASE >= 3:
        attn_b(b)
    if N_PHASE >= 4:
        moe(b, io["x3own"], io["y"], 2048, gather_idx=None)
    if N_PHASE < 4:
        stub = nc.alloc_sbuf_tensor("ystub", [128, 16], F32)
        b.op("vector", "memset", stub[:, :], 0.0, sem=b.VV)
        b.wait("sync", (b.VV, b.VV.n))
        b.dma("sync", y[0:128, 0:16], stub[:, :], key="ystub")
    b.fence(("sync",))
    b.finalize()
    return nc



def _core_consts(half):
    t = np.arange(S, dtype=np.float64)
    d = np.arange(128)
    invfa = 1.0 / 10000.0 ** (np.arange(0, DA, 2, dtype=np.float32)
                              .astype(np.float64) / DA)
    ang_a = invfa[(d % 64)][:, None] * t[None, :]
    cosa = np.cos(ang_a).astype(np.float32)
    sina = np.sin(ang_a).astype(np.float32)
    thb = 1.0 / 10000.0 ** (np.arange(0, DB, 2, dtype=np.float32)
                            .astype(np.float64) / DB)
    ang_b = thb[(d % 32)][:, None] * t[None, :]
    cosb = np.cos(ang_b).astype(np.float32)
    sinb = np.sin(ang_b).astype(np.float32)
    own = np.arange(half * 2048, half * 2048 + 2048)
    cosbq = np.ascontiguousarray(cosb[:, own])
    sinbq = np.ascontiguousarray(sinb[:, own])

    PaT = np.zeros((128, 128), np.float32)
    for i in range(64):
        PaT[2 * i + 1, 2 * i] = -1.0
        PaT[2 * i, 2 * i + 1] = 1.0
    PbT = np.zeros((128, 128), np.float32)
    for hh in range(2):
        o = hh * 64
        for i in range(32):
            PbT[o + i + 32, o + i] = -1.0
            PbT[o + i, o + i + 32] = 1.0

    p = np.arange(128)[:, None]
    j256 = np.arange(256)[None, :]
    trilTa = np.concatenate([(j256 >= p), (j256 >= p + 128)],
                            axis=1).astype(BF16NP)
    u = np.arange(6656)[None, :]
    maskb = (p <= u - 4096 + half * 2048).astype(BF16NP)
    idx = (half * 2048 + np.arange(16)[None, :] * 128
           + np.arange(128)[:, None]).astype(np.int32)
    return {
        "cosa": cosa, "sina": sina, "cosb": cosb, "sinb": sinb,
        "cosbq": cosbq, "sinbq": sinbq,
        "PaT": PaT.astype(BF16NP), "PbT": PbT.astype(BF16NP),
        "trilTa": trilTa, "maskb": maskb, "idx": idx,
        "ident": np.eye(128, dtype=BF16NP),
        "ones_col": np.ones((128, 1), BF16NP),
        "ones_row": np.ones((1, 128), np.float32),
    }


def kernel(L0_attn_a_x, L0_attn_a_W_qkv, L0_attn_a_W_out,
           L0_attn_b_Wq, L0_attn_b_Wk, L0_attn_b_Wv, L0_attn_b_Wo,
           L0_ffn_gate_w, L0_ffn_w_gate, L0_ffn_w_up, L0_ffn_w_down):
    x = np.asarray(L0_attn_a_x, dtype=np.float32)
    shared = {
        "wqkvT": np.ascontiguousarray(np.asarray(L0_attn_a_W_qkv,
                                                 np.float32).T),
        "woutT": np.ascontiguousarray(np.asarray(L0_attn_a_W_out,
                                                 np.float32).T),
        "wq": None,
        "wk": np.ascontiguousarray(np.asarray(L0_attn_b_Wk, np.float32)),
        "wv": np.ascontiguousarray(np.asarray(L0_attn_b_Wv, np.float32)),
        "wo": None,
        "gatewT": np.ascontiguousarray(np.asarray(L0_ffn_gate_w,
                                                  np.float32).T),
        "wgT": np.ascontiguousarray(
            np.asarray(L0_ffn_w_gate, np.float32).transpose(0, 2, 1)),
        "wuT": np.ascontiguousarray(
            np.asarray(L0_ffn_w_up, np.float32).transpose(0, 2, 1)),
        "wdT": np.ascontiguousarray(
            np.asarray(L0_ffn_w_down, np.float32).transpose(0, 2, 1)),
    }
    perm = np.empty(2048, np.int64)
    for c_ in range(16):
        for o01 in range(2):
            h_ = (c_ // 4) * 8 + (c_ % 4) + 4 * o01
            perm[c_ * 128 + o01 * 64: c_ * 128 + o01 * 64 + 64] = \
                np.arange(h_ * 64, h_ * 64 + 64)
    shared["wq"] = np.ascontiguousarray(
        np.asarray(L0_attn_b_Wq, np.float32)[:, perm])
    shared["wo"] = np.ascontiguousarray(
        np.asarray(L0_attn_b_Wo, np.float32)[perm, :])
    consts = [_core_consts(hf) for hf in (0, 1)]
    in_maps = []
    for core in range(8):
        bi, hf = core // 2, core % 2
        m = dict(shared)
        m.update(consts[hf])
        m["x0"] = np.ascontiguousarray(x[bi])
        in_maps.append(m)

    nc = build_nc()
    res = run_bass_kernel_spmd(nc, in_maps, core_ids=list(range(8)))
    kernel.results = res.results
    out = np.empty((B, S, D), np.float32)
    for core in range(8):
        bi, hf = core // 2, core % 2
        out[bi, hf * 2048:(hf + 1) * 2048] = res.results[core]["y"]
    return out.reshape(B, S, D)
